# version 04
# baseline (speedup 1.0000x reference)
import numpy as np
import ml_dtypes
import concourse.bass as bass
import concourse.mybir as mybir
from concourse.bass_utils import run_bass_kernel_spmd
from contextlib import ExitStack

F32 = mybir.dt.float32
BF16 = mybir.dt.bfloat16
I32 = mybir.dt.int32
AF = mybir.ActivationFunctionType
OP = mybir.AluOpType
AX = mybir.AxisListType

MAGIC = 12582912.0
T_TOK = 4096
D = 1024
INNER = 4096
NT = T_TOK // 128
GRP = 256
NG = T_TOK // GRP
SUP = 256
NSUP = T_TOK // SUP


class SemC:
    def __init__(self, name):
        self.name = name
        self.h = None
        self.n = 0


def build_program():
    nc = bass.Bass("TRN2", target_bir_lowering=False)

    d_x = nc.dram_tensor("x", [T_TOK, D], F32, kind="ExternalInput").ap()
    d_Wg = nc.dram_tensor("Wg", [D, D], F32, kind="ExternalInput").ap()
    d_Wv = nc.dram_tensor("Wv", [D, D], F32, kind="ExternalInput").ap()
    d_Wd = nc.dram_tensor("Wd", [D, D], F32, kind="ExternalInput").ap()
    d_bias = nc.dram_tensor("biasT", [128, 24], F32, kind="ExternalInput").ap()
    d_wgate = nc.dram_tensor("wgate", [INNER, D], F32, kind="ExternalInput").ap()
    d_wup = nc.dram_tensor("wup", [INNER, D], F32, kind="ExternalInput").ap()
    d_wdown = nc.dram_tensor("wdown", [D, INNER], F32, kind="ExternalInput").ap()
    d_consts = nc.dram_tensor("consts", [128, 16], F32, kind="ExternalInput").ap()
    d_idf = nc.dram_tensor("identf", [128, 128], F32, kind="ExternalInput").ap()
    d_idb = nc.dram_tensor("identb", [128, 128], BF16, kind="ExternalInput").ap()
    d_ones = nc.dram_tensor("ones128", [128, 1], F32, kind="ExternalInput").ap()
    d_onesT = nc.dram_tensor("onesT", [1, 128], F32, kind="ExternalInput").ap()
    d_out = nc.dram_tensor("out", [T_TOK, D], F32, kind="ExternalOutput").ap()

    scr_acc = nc.dram_tensor("scr_acc", [T_TOK, D], F32, kind="Internal").ap()
    scr_xqF = nc.dram_tensor("scr_xqF", [8, 128, T_TOK], BF16, kind="Internal").ap()
    scr_pqF = nc.dram_tensor("scr_pqF", [NT, 128, INNER], BF16, kind="Internal").ap()
    scr_wdT = nc.dram_tensor("scr_wdT", [32, 128, D], BF16, kind="Internal").ap()

    es = ExitStack()

    def sb(name, shape, dt):
        return es.enter_context(nc.sbuf_tensor(name, shape, dt))

    def ps(name, shape, dt):
        return es.enter_context(nc.psum_tensor(name, shape, dt))

    with es:
        wqg = sb("wqg", [128, 8 * INNER], BF16)
        wqu = sb("wqu", [128, 8 * INNER], BF16)
        x_grp = [sb("x_grp0", [128, 2 * D], F32), sb("x_grp1", [128, 2 * D], F32)]
        sgb = sb("sgb", [128, GRP], F32)
        thb = sb("thb", [128, GRP], F32)
        sdb = sb("sdb", [128, GRP], F32)
        xsb = sb("xsb", [128, GRP], F32)
        ab = sb("ab", [128, GRP], F32)
        hb = sb("hb", [128, GRP], F32)
        carry = sb("carry", [128, 8], F32)
        qt = sb("qt", [128, 512], F32)
        qcm = sb("qcm", [128, 512], F32)
        qnd = sb("qnd", [128, 512], F32)
        qg1 = sb("qg1", [128, 512], F32)
        nhalf = sb("nhalf", [128, 512], F32)
        phalf = sb("phalf", [128, 512], F32)
        qci = sb("qci", [128, 512], I32)
        qtmp = sb("qtmp", [128, D], BF16)
        meanall = sb("meanall", [128, NT], F32)
        varall = sb("varall", [128, NT], F32)
        rstdall = sb("rstdall", [128, NT], F32)
        dxall = sb("dxall", [128, NT], F32)
        dpwall = sb("dpwall", [128, NT], F32)
        wsc = sb("wsc", [128, 8], F32)
        consts_sb = sb("consts_sb", [128, 16], F32)
        bias_sb = sb("bias_sb", [128, 24], F32)
        idf = sb("idf_sb", [128, 128], F32)
        idb = sb("idb_sb", [128, 128], BF16)
        ones128 = sb("ones128_sb", [128, 1], F32)
        onesT = sb("onesT_sb", [1, 128], F32)
        scal = sb("scal", [128, 8], F32)
        tin1 = sb("tin1", [1, 8], F32)
        pbuf = sb("pbuf", [128, INNER], F32)
        sg_slab = sb("sg_slab", [128, 512], F32)
        pqroll = sb("pqroll", [128, 1024], BF16)
        pqstage = sb("pqstage", [128, INNER], BF16)
        xqF_t = [sb("xqF_t0", [128, 8 * 128], BF16), sb("xqF_t1", [128, 8 * 128], BF16)]
        dvecs = sb("dvecs", [128, 4], F32)
        wchunk = [pbuf[:, j * D:(j + 1) * D] for j in range(4)]

        def accsup(c0, w):
            return pbuf[:, c0:c0 + w]

        ATTN_BASE = {"g": 0, "v": 8192, "d": 16384}

        def attn_lhs(mat, k, m):
            b = ATTN_BASE[mat] + k * D + m * 128
            return wqg[:, b:b + 128]

        def xnF(c0, w):
            return wqu[:, c0:c0 + w]

        def xqF(c0, w):
            return wqu[:, 2048 + c0:2048 + c0 + w]

        def xnbf(c0, w):
            return wqu[:, 4096 + c0:4096 + c0 + w]

        def wdT_build(c0, w):
            return wqu[:, c0:c0 + w]

        def pqF_sup(c0, w):
            return wqg[:, c0:c0 + w]

        def wdT_str(j):
            return wqg[:, 8192 + j * D:8192 + (j + 1) * D]

        pb = [ps(f"pb{i}", [128, 512], F32) for i in range(6)]
        ptb = [ps("ptb0", [128, 1024], BF16), ps("ptb1", [128, 1024], BF16)]

        names = ["G", "T", "V", "S", "LA0", "LX0", "LX1", "LP0", "LP1",
                 "LQ0", "LQ1", "LB0", "LB1", "LW0", "LW1", "LW2", "LW3",
                 "LP2", "LP3", "LW4", "LW5", "LW6", "LW7",
                 "SA", "SX", "SW", "SP", "SO0", "SO1"]
        sems = {n: SemC(n) for n in names}
        for smc in sems.values():
            smc.h = es.enter_context(nc.semaphore(smc.name))
        G, T, V, S = sems["G"], sems["T"], sems["V"], sems["S"]
        LA0 = sems["LA0"]
        LX = [sems["LX0"], sems["LX1"]]
        LP = [sems["LP0"], sems["LP1"], sems["LP2"], sems["LP3"]]
        LQ = [sems["LQ0"], sems["LQ1"]]
        LB = [sems["LB0"], sems["LB1"]]
        LW = [sems[f"LW{j}"] for j in range(8)]
        SA, SX, SW, SP = sems["SA"], sems["SX"], sems["SW"], sems["SP"]
        SO = [sems["SO0"], sems["SO1"]]

        block = es.enter_context(nc.Block())
        plan = {"t": [], "v": [], "s": [], "g": [], "y": []}
        ENG_SEM = {"v": V, "s": S}

        def op(eng, build, sem=None, inc=0, wait=()):
            waits = list(wait)
            if eng in ("v", "s"):
                esem = ENG_SEM[eng]
                if esem.n > 0:
                    waits.append((esem, esem.n))
                sem, inc = esem, 1
            for (smc, val) in waits:
                if val > 0:
                    plan[eng].append(lambda e, _s=smc, _v=val: e.wait_ge(_s.h, _v))
            if sem is not None and inc:
                sem.n += inc

            def f(e, _b=build, _s=sem, _i=inc):
                ins = _b(e)
                if _s is not None and _i:
                    ins.then_inc(_s.h, _i)
            plan[eng].append(f)

        def wv(smc):
            return (smc, smc.n)

        C_NH = consts_sb[:, 0:1]
        C_PH = consts_sb[:, 1:2]
        C_MG = consts_sb[:, 2:3]
        C_NMG = consts_sb[:, 3:4]
        C_EPS = consts_sb[:, 4:5]

        def emit_round_neg(src_ap, s_vec_ap, dst_ap, clip1=False, pre_wait=()):
            fd = src_ap.shape[-1]
            tq, ci, cm, nd, b1 = (qt[:, 0:fd], qci[:, 0:fd], qcm[:, 0:fd],
                                  qnd[:, 0:fd], qg1[:, 0:fd])
            if s_vec_ap is None:
                op("v", lambda e, a=src_ap, o=tq: e.tensor_copy(o, a), wait=pre_wait)
            else:
                op("v", lambda e, a=src_ap, sv=s_vec_ap, o=tq:
                   e.tensor_scalar(o, a, sv, None, OP.mult), wait=pre_wait)
            if clip1:
                op("v", lambda e, a=nhalf[:, 0:fd], b=tq, o=b1:
                   e.tensor_tensor(o, a, b, OP.is_gt))
                op("v", lambda e, a=tq, b=phalf[:, 0:fd], o=cm:
                   e.tensor_tensor(o, a, b, OP.is_gt))
                op("v", lambda e, a=b1, b=cm, o=dst_ap: e.tensor_tensor(o, a, b, OP.subtract))
                return
            op("v", lambda e, a=tq, o=ci: e.tensor_scalar(o, a, MAGIC, None, OP.add))
            op("v", lambda e, a=ci, o=cm: e.tensor_scalar(o, a, -MAGIC, None, OP.add))
            op("v", lambda e, a=cm, b=tq, o=nd: e.tensor_tensor(o, a, b, OP.subtract))
            op("v", lambda e, a=nhalf[:, 0:fd], b=nd, o=b1:
               e.tensor_tensor(o, a, b, OP.is_gt))
            op("v", lambda e, a=nd, b=phalf[:, 0:fd], o=tq:
               e.tensor_tensor(o, a, b, OP.is_gt))
            op("v", lambda e, a=tq, b=b1, o=nd: e.tensor_tensor(o, a, b, OP.subtract))
            op("v", lambda e, a=nd, b=cm, o=dst_ap: e.tensor_tensor(o, a, b, OP.subtract))

        tp_state = {"n": 0, "marks": []}

        def emit_transpose4(srcs, dst_ap, src_wait=(), dst_wait=()):
            n = tp_state["n"]
            tp_state["n"] += 1
            bank = n % 2
            half = (n // 2) % 2
            base = half * 4
            twait = list(src_wait)
            if n - 2 >= 0:
                twait.append(tp_state["marks"][n - 2])
            for i, s4 in enumerate(srcs):
                op("t", lambda e, i_=s4, o=ptb[bank][:, (base + i) * 128:(base + i + 1) * 128]:
                   e.transpose(o, i_, idb[:]),
                   sem=T, inc=(1 if i == 3 else 0),
                   wait=tuple(twait) if i == 0 else ())
            op("s", lambda e, i_=ptb[bank][:, base * 128:(base + 4) * 128], o=dst_ap:
               e.activation(o, i_, AF.Copy, bias=0.0, scale=1.0),
               wait=(wv(T),) + tuple(dst_wait))
            tp_state["marks"].append(wv(S))

        for dst, src in ((consts_sb, d_consts), (idf, d_idf), (idb, d_idb),
                         (ones128, d_ones), (onesT, d_onesT), (bias_sb, d_bias)):
            op("g", lambda e, o=dst, i=src: e.dma_start(out=o[:], in_=i), sem=G, inc=16)
        for nm, src in (("g", d_Wg), ("v", d_Wv), ("d", d_Wd)):
            for k in range(8):
                b = ATTN_BASE[nm] + k * D
                op("g", lambda e, o=wqg[:, b:b + D], i=src[k * 128:(k + 1) * 128, :]:
                   e.dma_start(out=o, in_=i), sem=G, inc=16)
        g_after_consts = G.n

        op("v", lambda e: e.memset(carry[:], 0.0))
        op("v", lambda e: e.memset(nhalf[:], -0.5))
        op("v", lambda e: e.memset(phalf[:], 0.5))

        for i in range(NT):
            buf, sq = wchunk[0], wchunk[1]
            op("y", lambda e, o=buf, i_=d_x[i * 128:(i + 1) * 128, :]:
               e.dma_start(out=o[:], in_=i_), sem=LA0, inc=16, wait=(wv(V), wv(S)))
            ldw = wv(LA0)
            op("v", lambda e, i_=buf, o=qt[:, i:i + 1]:
               e.tensor_reduce(o, i_[:], AX.X, OP.add), wait=(ldw,))
            op("s", lambda e, i_=buf, o=sq: e.activation(o[:], i_[:], AF.Square, bias=0.0, scale=1.0),
               wait=(ldw, wv(V)))
            op("v", lambda e, i_=sq, o=qcm[:, i:i + 1]:
               e.tensor_reduce(o, i_[:], AX.X, OP.add), wait=(wv(S),))
        op("v", lambda e: e.tensor_scalar(meanall[:], qt[:, 0:NT], 1.0 / D, None, OP.mult))
        op("v", lambda e: e.tensor_scalar(qnd[:, 0:NT], qcm[:, 0:NT], 1.0 / D, None, OP.mult))
        op("v", lambda e: e.tensor_tensor(qg1[:, 0:NT], meanall[:], meanall[:], OP.mult))
        op("v", lambda e: e.tensor_tensor(varall[:], qnd[:, 0:NT], qg1[:, 0:NT], OP.subtract))
        op("s", lambda e: e.activation(qt[:, 32:32 + NT], varall[:], AF.Sqrt, bias=C_EPS, scale=1.0),
           wait=(wv(V),))
        op("v", lambda e: e.reciprocal(qcm[:, 32:32 + NT], qt[:, 32:32 + NT]), wait=(wv(S),))
        op("v", lambda e: e.tensor_tensor(qnd[:, 32:32 + NT], qcm[:, 32:32 + NT],
                                          qcm[:, 32:32 + NT], OP.mult))
        op("v", lambda e: e.tensor_scalar(qg1[:, 32:32 + NT], varall[:], 1e-5, None, OP.add))
        op("v", lambda e: e.tensor_tensor(qt[:, 64:64 + NT], qnd[:, 32:32 + NT],
                                          qg1[:, 32:32 + NT], OP.mult))
        op("v", lambda e: e.tensor_scalar(qcm[:, 64:64 + NT], qt[:, 64:64 + NT],
                                          -0.5, 1.5, OP.mult, OP.add))
        op("v", lambda e: e.tensor_tensor(rstdall[:], qcm[:, 32:32 + NT],
                                          qcm[:, 64:64 + NT], OP.mult))

        sa_group = {}
        sx_group = {}
        hbank_mark = [None, None]
        act_marks = [None, None]
        for g in range(NG):
            xg = x_grp[g % 2]
            lx = LX[g % 2]
            for tt in range(2):
                op("y", lambda e, o=xg[:, tt * D:(tt + 1) * D],
                   i_=d_x[g * GRP + tt * 128:g * GRP + (tt + 1) * 128, :]:
                   e.dma_start(out=o, in_=i_),
                   sem=lx, inc=16, wait=(wv(V), wv(T)) if tt == 0 else ())
            ld_x = wv(lx)
            if g > 0:
                op("v", lambda e: e.nop(), wait=((SA, sa_group[g - 1]), (SX, sx_group[g - 1])))
                op("s", lambda e: e.nop(), wait=((SA, sa_group[g - 1]), (SX, sx_group[g - 1])))

            for tt in range(2):
                idx = 2 * g + tt
                xtile = xg[:, tt * D:(tt + 1) * D]
                xnb = xnbf(tt * D, D)
                op("v", lambda e, i_=xtile, o=xnb, m=meanall[:, idx:idx + 1],
                   r=rstdall[:, idx:idx + 1]:
                   e.tensor_scalar(o, i_, m, r, OP.subtract, OP.mult),
                   wait=(ld_x, wv(T)))
                mx, mc, rec, sv = scal[:, 0:1], scal[:, 1:2], scal[:, 2:3], scal[:, 3:4]
                op("v", lambda e, i_=xnb, o=mx:
                   e.tensor_reduce(o, i_, AX.X, OP.max, apply_absolute_value=True))
                op("v", lambda e, i_=mx, o=mc: e.tensor_scalar(o, i_, 1e-5, None, OP.max))
                op("v", lambda e, i_=mc, o=rec: e.reciprocal(o, i_))
                op("v", lambda e, i_=rec, o=sv: e.tensor_scalar(o, i_, 127.0, None, OP.mult))
                op("v", lambda e, i_=mc, o=dxall[:, idx:idx + 1]:
                   e.tensor_scalar(o, i_, 1.0 / 127.0, None, OP.mult))
                for pc in range(2):
                    emit_round_neg(xnb[:, pc * 512:(pc + 1) * 512], sv,
                                   pqstage[:, tt * D + pc * 512:tt * D + (pc + 1) * 512],
                                   pre_wait=(wv(T),))
            srcw = (wv(V),)
            for kp in range(4):
                k = 2 * kp
                emit_transpose4(
                    [xnbf(0 * D + k * 128, 128), xnbf(1 * D + k * 128, 128),
                     xnbf(0 * D + (k + 1) * 128, 128), xnbf(1 * D + (k + 1) * 128, 128)],
                    xnF(k * GRP, 512), src_wait=srcw)
            for kp in range(4):
                k = 2 * kp
                emit_transpose4(
                    [pqstage[:, 0 * D + k * 128:0 * D + (k + 1) * 128],
                     pqstage[:, 1 * D + k * 128:1 * D + (k + 1) * 128],
                     pqstage[:, 0 * D + (k + 1) * 128:0 * D + (k + 2) * 128],
                     pqstage[:, 1 * D + (k + 1) * 128:1 * D + (k + 2) * 128]],
                    xqF(k * GRP, 512), src_wait=srcw)

            xn_evac_marks = (wv(V), wv(S))
            sd_mark = None
            eps_mark = None
            scan_marks = {}

            def emit_htrans_stage(mh):
                for tt in range(2):
                    pdst2 = pb[4 + tt][:, 0:128]
                    tw = [scan_marks[mh]]
                    if hbank_mark[tt] is not None:
                        tw.append(hbank_mark[tt])
                    op("t", lambda e, i_=hb[:, tt * 128:(tt + 1) * 128], o=pdst2:
                       e.transpose(o, i_, idf[:]), sem=T, inc=1, wait=tuple(tw))
                    hmark = wv(T)
                    stage = (qt, qcm, qnd, qg1)[tt * 2 + (mh // 4)]
                    scol = (mh % 4) * 128
                    op("v", lambda e, i_=pdst2, o=stage[:, scol:scol + 128],
                       x_=xg[:, tt * D + mh * 128: tt * D + mh * 128 + 128]:
                       e.tensor_tensor(o, i_, x_, OP.add), wait=(hmark,))
                    hbank_mark[tt] = wv(V)

            def emit_acc_dma(h2):
                vd = wv(V)
                for tt in range(2):
                    stage = (qt, qcm, qnd, qg1)[tt * 2 + h2]
                    rows = g * GRP + tt * 128
                    op("y", lambda e, i_=stage,
                       o=scr_acc[rows:rows + 128, h2 * 512:(h2 + 1) * 512]:
                       e.dma_start(out=o, in_=i_[:]), sem=SA, inc=16, wait=(vd,))

            for m in range(8):
                par = m % 2
                pg = pb[2 * par][:, 0:GRP]
                pv = pb[2 * par][:, GRP:2 * GRP]
                pd = pb[2 * par + 1][:, 0:GRP]
                mm_wait = list(xn_evac_marks) if m < 2 else []
                if act_marks[par] is not None:
                    mm_wait.append(act_marks[par])
                mm_wait = tuple(mm_wait)
                first = True
                for mat, pdst in (("g", pg), ("v", pv), ("d", pd)):
                    for k in range(8):
                        op("t", lambda e, l=attn_lhs(mat, k, m),
                           r=xnF(k * GRP, GRP), o=pdst, st=(k == 0), sp=(k == 7):
                           e.matmul(o, l, r, start=st, stop=sp),
                           sem=T, inc=(1 if k == 7 else 0),
                           wait=mm_wait if first else ())
                        first = False
                t_gvd = wv(T)
                if m >= 1:
                    emit_htrans_stage(m - 1)
                    if m == 4:
                        emit_acc_dma(0)
                op("s", lambda e, i_=pg, o=sgb[:, 0:GRP], b=bias_sb[:, m:m + 1]:
                   e.activation(o, i_, AF.Sigmoid, bias=b, scale=1.0),
                   wait=(t_gvd,) if eps_mark is None else (t_gvd, eps_mark))
                op("s", lambda e, i_=pv, o=thb[:, 0:GRP], b=bias_sb[:, 8 + m:9 + m]:
                   e.activation(o, i_, AF.Tanh, bias=b, scale=1.0))
                op("s", lambda e, i_=pd, o=sdb[:, 0:GRP], b=bias_sb[:, 16 + m:17 + m]:
                   e.activation(o, i_, AF.Sigmoid, bias=b, scale=1.0))
                sd_mark = wv(S)
                act_marks[par] = sd_mark
                op("v", lambda e: e.tensor_tensor(xsb[:, 0:GRP], sgb[:, 0:GRP],
                                                  thb[:, 0:GRP], OP.mult), wait=(sd_mark,))
                op("v", lambda e: e.tensor_scalar(ab[:, 0:GRP], sdb[:, 0:GRP],
                                                  0.998, 0.001, OP.mult, OP.add))
                eps_mark = wv(V)
                op("v", lambda e, c=carry[:, m:m + 1]: e.tensor_tensor_scan(
                    hb[:, 0:GRP], ab[:, 0:GRP], xsb[:, 0:GRP], c, OP.mult, OP.add))
                op("v", lambda e, c=carry[:, m:m + 1]: e.tensor_copy(c, hb[:, GRP - 1:GRP]))
                scan_marks[m] = wv(V)
            emit_htrans_stage(7)
            emit_acc_dma(1)

            v_done, s_done = wv(V), wv(S)
            for k in range(8):
                op("y", lambda e, i_=xqF(k * GRP, GRP), o=scr_xqF[k, :, g * GRP:(g + 1) * GRP]:
                   e.dma_start(out=o, in_=i_), sem=SX, inc=16,
                   wait=(s_done, v_done) if k == 0 else ())
            sa_group[g] = SA.n
            sx_group[g] = SX.n

        op("v", lambda e: e.nop(), wait=((SA, sa_group[NG - 1]), (SX, sx_group[NG - 1])))
        op("s", lambda e: e.nop(), wait=((SA, sa_group[NG - 1]), (SX, sx_group[NG - 1])))

        def prep_scalars(sw_col, dw_col, inv_n):
            op("t", lambda e: e.matmul(pb[5][0:1, 0:1], ones128[:], scal[:, 4:5],
                                       start=True, stop=True), sem=T, inc=1, wait=(wv(V),))
            op("v", lambda e: e.tensor_copy(tin1[0:1, 0:1], pb[5][0:1, 0:1]), wait=(wv(T),))
            op("v", lambda e: e.tensor_scalar(tin1[0:1, 1:2], tin1[0:1, 0:1], inv_n, None, OP.mult))
            op("v", lambda e: e.tensor_scalar(tin1[0:1, 2:3], tin1[0:1, 1:2], 1e-5, None, OP.max))
            op("v", lambda e: e.reciprocal(tin1[0:1, 3:4], tin1[0:1, 2:3]))
            op("v", lambda e: e.reciprocal(tin1[0:1, 4:5], tin1[0:1, 3:4]))
            for srcc, col in ((3, sw_col), (4, dw_col)):
                op("t", lambda e, i_=tin1[0:1, srcc:srcc + 1]:
                   e.matmul(pb[5][:, 1:2], onesT[:], i_, start=True, stop=True),
                   sem=T, inc=1, wait=(wv(V),))
                op("v", lambda e, o=wsc[:, col:col + 1]: e.tensor_copy(o, pb[5][:, 1:2]),
                   wait=(wv(T),))

        def prep_mean(load_list):
            npart = len(load_list) * 4
            assert npart <= 512
            vmarks = {}
            for li, src_slice in enumerate(load_list):
                buf = wchunk[li % 4]
                lp = LP[li % 4]
                lw_ = (vmarks[li - 4],) if li >= 4 else (wv(V),)
                op("y", lambda e, o=buf, i_=src_slice: e.dma_start(out=o, in_=i_),
                   sem=lp, inc=16, wait=lw_)
                ldw = wv(lp)
                for pc in range(4):
                    col = li * 4 + pc
                    op("v", lambda e, i_=buf[:, pc * 256:(pc + 1) * 256], o=qg1[:, col:col + 1]:
                       e.tensor_reduce(o, i_, AX.X, OP.add, apply_absolute_value=True),
                       wait=(ldw,) if pc == 0 else ())
                vmarks[li] = wv(V)
            op("v", lambda e, w=npart: e.tensor_reduce(scal[:, 4:5], qg1[:, 0:w], AX.X, OP.add))

        def qtmp4(cc, c0, w):
            return pqstage[:, cc * D + c0:cc * D + c0 + w]

        def prep_quant(load_list, sw_col, dest4_fn, pre_first=()):
            vmarks = {}
            tmark_prev_batch = None
            for bi in range(len(load_list) // 4):
                batch = load_list[bi * 4:(bi + 1) * 4]
                for cc, (src_slice, _) in enumerate(batch):
                    li = bi * 4 + cc
                    buf = wchunk[li % 4]
                    lp = LP[li % 4]
                    lw_ = (vmarks[li - 4],) if li >= 4 else (wv(V),)
                    op("y", lambda e, o=buf, i_=src_slice: e.dma_start(out=o, in_=i_),
                       sem=lp, inc=16, wait=lw_)
                    ldw = wv(lp)
                    for pc in range(2):
                        pw = [ldw] if pc == 0 else []
                        if pc == 0 and tmark_prev_batch is not None:
                            pw.append(tmark_prev_batch)
                        if li == 0:
                            pw.extend(pre_first)
                        emit_round_neg(buf[:, pc * 512:(pc + 1) * 512],
                                       wsc[:, sw_col:sw_col + 1],
                                       qtmp4(cc, pc * 512, 512), clip1=True,
                                       pre_wait=tuple(pw))
                    vmarks[li] = wv(V)
                srcw = (wv(V),)
                metas = [m for (_, m) in batch]
                for kb in range(8):
                    dest4_fn(metas, kb, srcw)
                tmark_prev_batch = wv(T)

        down_l1 = [d_wdown[c * 128:(c + 1) * 128, q * D:(q + 1) * D]
                   for c in range(8) for q in range(4)]
        prep_mean(down_l1)
        prep_scalars(4, 5, 1.0 / (D * INNER))
        down_l2 = [(d_wdown[(cb * 4 + cc) * 128:(cb * 4 + cc + 1) * 128, q * D:(q + 1) * D],
                    (cb * 4 + cc, q))
                   for q in range(4) for cb in range(2) for cc in range(4)]

        def down_dest4(metas, kb, srcw):
            c0, q = metas[0]
            kglob = q * 8 + kb
            emit_transpose4([qtmp4(cc, kb * 128, 128) for cc in range(4)],
                            wdT_build(kglob * D + c0 * 128, 512), src_wait=srcw)
        prep_quant(down_l2, 4, down_dest4)
        v_done, s_done = wv(V), wv(S)
        for k in range(32):
            op("y", lambda e, i_=wdT_build(k * D, D), o=scr_wdT[k]:
               e.dma_start(out=o, in_=i_), sem=SW, inc=16,
               wait=(v_done, s_done) if k == 0 else ())
        st_wdT = SW.n

        gate_l1 = [d_wgate[c * 128:(c + 1) * 128, :] for c in range(32)]
        prep_mean(gate_l1)
        prep_scalars(0, 1, 1.0 / (INNER * D))
        gate_l2 = [(d_wgate[c * 128:(c + 1) * 128, :], c) for c in range(32)]

        def gate_dest4(metas, kb, srcw):
            c0 = metas[0]
            emit_transpose4([qtmp4(cc, kb * 128, 128) for cc in range(4)],
                            wqg[:, kb * INNER + c0 * 128: kb * INNER + c0 * 128 + 512],
                            src_wait=srcw)
        prep_quant(gate_l2, 0, gate_dest4)

        up_l1 = [d_wup[c * 128:(c + 1) * 128, :] for c in range(32)]
        prep_mean(up_l1)
        prep_scalars(2, 3, 1.0 / (INNER * D))
        up_l2 = [(d_wup[c * 128:(c + 1) * 128, :], c) for c in range(32)]

        def up_dest4(metas, kb, srcw):
            c0 = metas[0]
            emit_transpose4([qtmp4(cc, kb * 128, 128) for cc in range(4)],
                            wqu[:, kb * INNER + c0 * 128: kb * INNER + c0 * 128 + 512],
                            src_wait=srcw)
        prep_quant(up_l2, 2, up_dest4, pre_first=((SW, st_wdT),))

        slab_ctr = {"n": 0}
        st_pq = {}
        t_after_tile = {}
        bank_marks = [None] * 6
        roll_marks = []
        pevac_mark = None
        for i in range(NT):
            xqt = xqF_t[i % 2]
            lq = LQ[i % 2]
            xq_wait = [(SX, sx_group[i // 2])]
            if i >= 2:
                xq_wait.append((T, t_after_tile[i - 2]))
            else:
                xq_wait.append(wv(T))
            for k in range(8):
                op("y", lambda e, o=xqt[:, k * 128:(k + 1) * 128],
                   i_=scr_xqF[k, :, i * 128:(i + 1) * 128]:
                   e.dma_start(out=o, in_=i_), sem=lq, inc=16,
                   wait=tuple(xq_wait) if k == 0 else ())
            ld_xq = wv(lq)
            op("v", lambda e, o=dvecs[:, 0:1], a=dxall[:, i:i + 1], b=wsc[:, 1:2]:
               e.tensor_tensor(o, a, b, OP.mult), wait=(wv(S),))
            op("v", lambda e, o=dvecs[:, 1:2], a=dxall[:, i:i + 1], b=wsc[:, 3:4]:
               e.tensor_tensor(o, a, b, OP.mult))
            dv_mark = wv(V)
            if i > 0:
                op("v", lambda e: e.nop(), wait=((SP, st_pq[i - 1]),))
                op("s", lambda e: e.nop(), wait=((SP, st_pq[i - 1]),))

            for s_ in range(8):
                bkg = (2 * slab_ctr["n"]) % 6
                bku = (2 * slab_ctr["n"] + 1) % 6
                slab_ctr["n"] += 1
                pg_, pu_ = pb[bkg][:], pb[bku][:]
                mmw = [ld_xq] if s_ == 0 else []
                for bk in (bkg, bku):
                    if bank_marks[bk] is not None:
                        mmw.append(bank_marks[bk])
                for k in range(8):
                    bg0 = k * INNER + s_ * 512
                    op("t", lambda e, l=xqt[:, k * 128:(k + 1) * 128],
                       r=wqg[:, bg0:bg0 + 512], o=pg_, st=(k == 0), sp=(k == 7):
                       e.matmul(o, l, r, start=st, stop=sp),
                       sem=T, inc=(1 if k == 7 else 0), wait=tuple(mmw) if k == 0 else ())
                    op("t", lambda e, l=xqt[:, k * 128:(k + 1) * 128],
                       r=wqu[:, bg0:bg0 + 512], o=pu_, st=(k == 0), sp=(k == 7):
                       e.matmul(o, l, r, start=st, stop=sp),
                       sem=T, inc=(1 if k == 7 else 0))
                tmm = wv(T)
                sw = [tmm]
                if pevac_mark is not None:
                    sw.append(pevac_mark)
                if s_ == 0:
                    sw.append(dv_mark)
                op("s", lambda e, i_=pg_, o=sg_slab[:], sc=dvecs[:, 0:1]:
                   e.activation(o, i_, AF.Silu, bias=0.0, scale=sc), wait=tuple(sw))
                silu_mark = wv(S)
                bank_marks[bkg] = silu_mark
                op("v", lambda e, i_=pu_, o=pbuf[:, s_ * 512:(s_ + 1) * 512], sc=dvecs[:, 1:2]:
                   e.scalar_tensor_tensor(o, i_, sc, sg_slab[:], OP.mult, OP.mult),
                   wait=(silu_mark,))
                pevac_mark = wv(V)
                bank_marks[bku] = pevac_mark
            mxp, mcp, recp, svp = scal[:, 6:7], scal[:, 7:8], scal[:, 0:1], scal[:, 1:2]
            op("v", lambda e, o=mxp: e.tensor_reduce(o, pbuf[:], AX.X, OP.max,
                                                     apply_absolute_value=True))
            op("v", lambda e, i_=mxp, o=mcp: e.tensor_scalar(o, i_, 1e-5, None, OP.max))
            op("v", lambda e, i_=mcp, o=recp: e.reciprocal(o, i_))
            op("v", lambda e, i_=recp, o=svp: e.tensor_scalar(o, i_, 127.0, None, OP.mult))
            op("v", lambda e, i_=mcp, o=scal[:, 2:3]:
               e.tensor_scalar(o, i_, 1.0 / 127.0, None, OP.mult))
            op("v", lambda e, o=dpwall[:, i:i + 1], a=scal[:, 2:3], b=wsc[:, 5:6]:
               e.tensor_tensor(o, a, b, OP.mult))
            for pc in range(8):
                dst_roll = pqroll[:, (pc % 2) * 512:(pc % 2) * 512 + 512]
                pw = (roll_marks[-2],) if len(roll_marks) >= 2 else ()
                emit_round_neg(pbuf[:, pc * 512:(pc + 1) * 512], svp, dst_roll,
                               pre_wait=pw)
                emit_transpose4([dst_roll[:, b4 * 128:(b4 + 1) * 128] for b4 in range(4)],
                                pqstage[:, pc * 512:(pc + 1) * 512], src_wait=(wv(V),))
                roll_marks.append(wv(T))
            t_after_tile[i] = T.n
            v_done, s_done = wv(V), wv(S)
            op("y", lambda e, i_=pqstage, o=scr_pqF[i]:
               e.dma_start(out=o, in_=i_[:]), sem=SP, inc=16, wait=(v_done, s_done))
            st_pq[i] = SP.n

        b1_done_T = wv(T)
        b1_done_V = wv(V)
        t_tile_mark = {}
        t_k_mark = {}
        pair_marks = [None, None, None]
        so_marks = {0: [], 1: []}
        v_evac_marks = {}

        def wdT_str4(j):
            return wqg[:, 8192 + j * D:8192 + (j + 1) * D]

        for u in range(NSUP):
            tiles = (2 * u, 2 * u + 1)
            lb = LB[u % 2]
            pr0 = (2 * u) % 3
            pr1 = (2 * u + 1) % 3
            wl = [b1_done_T, b1_done_V, (SP, st_pq[tiles[1]])]
            if u > 0:
                wl.append((V, v_evac_marks[u - 1]))
                wl.append((T, t_tile_mark[(u - 1, 1)]))
            for j, ti in enumerate(tiles):
                op("y", lambda e, o=pqF_sup(j * INNER, INNER), i_=scr_pqF[ti]:
                   e.dma_start(out=o, in_=i_), sem=lb, inc=16,
                   wait=tuple(wl) if j == 0 else ())
            for tt2 in range(2):
                op("y", lambda e, o=accsup(tt2 * D, D),
                   i_=scr_acc[u * SUP + tt2 * 128:u * SUP + (tt2 + 1) * 128, :]:
                   e.dma_start(out=o, in_=i_),
                   sem=lb, inc=16, wait=((SA, sa_group[u]),) if tt2 == 0 else ())
            ld_sup = wv(lb)
            for k in range(32):
                kg = u * 32 + k
                wbuf = wdT_str4(kg % 8)
                lw = LW[kg % 8]
                w_wait = [(SW, st_wdT)]
                if kg - 8 >= 0:
                    uu, kk = divmod(kg - 8, 32)
                    w_wait.append((T, t_k_mark[(uu, kk)]))
                else:
                    w_wait.append(b1_done_T)
                op("y", lambda e, o=wbuf, i_=scr_wdT[k]: e.dma_start(out=o, in_=i_),
                   sem=lw, inc=16, wait=tuple(w_wait))
                ld_w = wv(lw)
                for j in range(2):
                    pr = pr0 if j == 0 else pr1
                    for h in range(2):
                        pdst = pb[2 * pr + h][:]
                        mmw = []
                        if k == 0 and h == 0:
                            mmw = [ld_sup, ld_w]
                            if pair_marks[pr] is not None:
                                mmw.append(pair_marks[pr])
                        elif j == 0 and h == 0:
                            mmw = [ld_w]
                        op("t", lambda e, l=pqF_sup(j * INNER + k * 128, 128),
                           r=wbuf[:, h * 512:(h + 1) * 512], o=pdst,
                           st=(k == 0), sp=(k == 31):
                           e.matmul(o, l, r, start=st, stop=sp),
                           sem=T, inc=(1 if h == 1 else 0),
                           wait=tuple(mmw))
                    if k == 31:
                        t_tile_mark[(u, j)] = T.n
                t_k_mark[(u, k)] = T.n
            for j, ti in enumerate(tiles):
                pr = pr0 if j == 0 else pr1
                tmark = (T, t_tile_mark[(u, j)])
                for h in range(2):
                    pidx = 2 * j + h
                    psrc = pb[2 * pr + h][:]
                    spar = pidx % 2
                    stg = sg_slab if spar == 0 else qg1
                    stw = [tmark]
                    if so_marks[spar]:
                        stw.append((SO[spar], so_marks[spar][-1]))
                    op("v", lambda e, i_=psrc, o=qt[:], sc=dpwall[:, ti:ti + 1]:
                       e.tensor_scalar(o, i_, sc, None, OP.mult), wait=tuple(stw))
                    op("v", lambda e, a=qt[:], b=accsup(j * D + h * 512, 512), o=stg[:]:
                       e.tensor_tensor(o, a, b, OP.add))
                    vmark = wv(V)
                    rows = u * SUP + j * 128
                    op("y", lambda e, i_=stg,
                       o=d_out[rows:rows + 128, h * 512:(h + 1) * 512]:
                       e.dma_start(out=o, in_=i_[:]), sem=SO[spar], inc=16, wait=(vmark,))
                    so_marks[spar].append(SO[spar].n)
                pair_marks[pr] = wv(V)
            v_evac_marks[u] = V.n

        fin0, fin1 = SO[0].n, SO[1].n
        op("y", lambda e, a=fin0, b=fin1: e.wait_ge(SO[0].h, a))
        op("y", lambda e, a=fin0, b=fin1: e.wait_ge(SO[1].h, b))

        @block.sync
        def _(e):
            for f in plan["y"]:
                f(e)

        @block.gpsimd
        def _(e):
            for f in plan["g"]:
                f(e)

        @block.vector
        def _(e):
            e.wait_ge(G.h, g_after_consts)
            for f in plan["v"]:
                f(e)

        @block.scalar
        def _(e):
            e.wait_ge(G.h, g_after_consts)
            for f in plan["s"]:
                f(e)

        @block.tensor
        def _(e):
            e.wait_ge(G.h, g_after_consts)
            for f in plan["t"]:
                f(e)

    return nc


_NC_CACHE = {}


def kernel(L0_attn_x, L0_attn_Wg, L0_attn_bg, L0_attn_Wv, L0_attn_bv,
           L0_attn_Wd, L0_attn_bd, L0_ffn_w_gate, L0_ffn_w_up, L0_ffn_w_down):
    x = np.ascontiguousarray(np.asarray(L0_attn_x, dtype=np.float32))
    B = x.shape[0]
    Wg = np.ascontiguousarray(np.asarray(L0_attn_Wg, np.float32))
    Wv = np.ascontiguousarray(np.asarray(L0_attn_Wv, np.float32))
    Wd = np.ascontiguousarray(np.asarray(L0_attn_Wd, np.float32))
    bg = np.asarray(L0_attn_bg, np.float32).reshape(8, 128).T
    bv = np.asarray(L0_attn_bv, np.float32).reshape(8, 128).T
    bd = np.asarray(L0_attn_bd, np.float32).reshape(8, 128).T
    biasT = np.ascontiguousarray(np.concatenate([bg, bv, bd], axis=1))
    wgate = np.ascontiguousarray(np.asarray(L0_ffn_w_gate, np.float32))
    wup = np.ascontiguousarray(np.asarray(L0_ffn_w_up, np.float32))
    wdown = np.ascontiguousarray(np.asarray(L0_ffn_w_down, np.float32))

    consts = np.zeros((128, 16), np.float32)
    consts[:, 0] = -0.5
    consts[:, 1] = 0.5
    consts[:, 2] = MAGIC
    consts[:, 3] = -MAGIC
    consts[:, 4] = 1e-5
    identf = np.eye(128, dtype=np.float32)
    identb = np.asarray(np.eye(128, dtype=np.float32).astype(ml_dtypes.bfloat16))
    ones128 = np.ones((128, 1), np.float32)
    onesT = np.ones((1, 128), np.float32)

    if "nc" not in _NC_CACHE:
        _NC_CACHE["nc"] = build_program()
    nc = _NC_CACHE["nc"]

    in_maps = []
    for b in range(B):
        in_maps.append(dict(
            x=np.ascontiguousarray(x[b]), Wg=Wg, Wv=Wv, Wd=Wd, biasT=biasT,
            wgate=wgate, wup=wup, wdown=wdown, consts=consts,
            identf=identf, identb=identb, ones128=ones128, onesT=onesT,
        ))
    res = run_bass_kernel_spmd(nc, in_maps, core_ids=list(range(B)))
    out = np.stack([np.asarray(r["out"], np.float32) for r in res.results], axis=0)
    return out


# version 05
# speedup vs baseline: 1.0034x; 1.0034x over previous
import numpy as np
import ml_dtypes
import concourse.bass as bass
import concourse.mybir as mybir
from concourse.bass_utils import run_bass_kernel_spmd
from contextlib import ExitStack

F32 = mybir.dt.float32
BF16 = mybir.dt.bfloat16
I32 = mybir.dt.int32
AF = mybir.ActivationFunctionType
OP = mybir.AluOpType
AX = mybir.AxisListType

MAGIC = 12582912.0
T_TOK = 4096
D = 1024
INNER = 4096
NT = T_TOK // 128
GRP = 256
NG = T_TOK // GRP
SUP = 256
NSUP = T_TOK // SUP


class SemC:
    def __init__(self, name):
        self.name = name
        self.h = None
        self.n = 0


def build_program():
    nc = bass.Bass("TRN2", target_bir_lowering=False)

    d_x = nc.dram_tensor("x", [T_TOK, D], F32, kind="ExternalInput").ap()
    d_Wg = nc.dram_tensor("Wg", [D, D], F32, kind="ExternalInput").ap()
    d_Wv = nc.dram_tensor("Wv", [D, D], F32, kind="ExternalInput").ap()
    d_Wd = nc.dram_tensor("Wd", [D, D], F32, kind="ExternalInput").ap()
    d_bias = nc.dram_tensor("biasT", [128, 24], F32, kind="ExternalInput").ap()
    d_wgate = nc.dram_tensor("wgate", [INNER, D], F32, kind="ExternalInput").ap()
    d_wup = nc.dram_tensor("wup", [INNER, D], F32, kind="ExternalInput").ap()
    d_wdown = nc.dram_tensor("wdown", [D, INNER], F32, kind="ExternalInput").ap()
    d_consts = nc.dram_tensor("consts", [128, 16], F32, kind="ExternalInput").ap()
    d_idf = nc.dram_tensor("identf", [128, 128], F32, kind="ExternalInput").ap()
    d_idb = nc.dram_tensor("identb", [128, 128], BF16, kind="ExternalInput").ap()
    d_ones = nc.dram_tensor("ones128", [128, 1], F32, kind="ExternalInput").ap()
    d_onesT = nc.dram_tensor("onesT", [1, 128], F32, kind="ExternalInput").ap()
    d_out = nc.dram_tensor("out", [T_TOK, D], F32, kind="ExternalOutput").ap()

    scr_acc = nc.dram_tensor("scr_acc", [T_TOK, D], F32, kind="Internal").ap()
    scr_xqF = nc.dram_tensor("scr_xqF", [8, 128, T_TOK], BF16, kind="Internal").ap()
    scr_pqF = nc.dram_tensor("scr_pqF", [NT, 128, INNER], BF16, kind="Internal").ap()
    scr_wdT = nc.dram_tensor("scr_wdT", [32, 128, D], BF16, kind="Internal").ap()

    es = ExitStack()

    def sb(name, shape, dt):
        return es.enter_context(nc.sbuf_tensor(name, shape, dt))

    def ps(name, shape, dt):
        return es.enter_context(nc.psum_tensor(name, shape, dt))

    with es:
        wqg = sb("wqg", [128, 8 * INNER], BF16)
        wqu = sb("wqu", [128, 8 * INNER], BF16)
        x_grp = [sb("x_grp0", [128, 2 * D], F32), sb("x_grp1", [128, 2 * D], F32)]
        epi = sb("epi", [128, 6 * GRP], F32)
        sgb = epi[:, 0 * GRP:1 * GRP]
        thb = epi[:, 1 * GRP:2 * GRP]
        sdb = epi[:, 2 * GRP:3 * GRP]
        xsb = epi[:, 3 * GRP:4 * GRP]
        ab = epi[:, 4 * GRP:5 * GRP]
        hb = epi[:, 5 * GRP:6 * GRP]
        carry = sb("carry", [128, 8], F32)
        qt = sb("qt", [128, 512], F32)
        qcm = sb("qcm", [128, 512], F32)
        qnd = sb("qnd", [128, 512], F32)
        qg1 = sb("qg1", [128, 512], F32)
        nhalf = sb("nhalf", [128, 512], F32)
        phalf = sb("phalf", [128, 512], F32)
        qci = sb("qci", [128, 512], I32)
        qtmp = sb("qtmp", [128, D], BF16)
        meanall = sb("meanall", [128, NT], F32)
        varall = sb("varall", [128, NT], F32)
        rstdall = sb("rstdall", [128, NT], F32)
        dxall = sb("dxall", [128, NT], F32)
        dpwall = sb("dpwall", [128, NT], F32)
        wsc = sb("wsc", [128, 8], F32)
        consts_sb = sb("consts_sb", [128, 16], F32)
        bias_sb = sb("bias_sb", [128, 24], F32)
        idf = sb("idf_sb", [128, 128], F32)
        idb = sb("idb_sb", [128, 128], BF16)
        ones128 = sb("ones128_sb", [128, 1], F32)
        onesT = sb("onesT_sb", [1, 128], F32)
        scal = sb("scal", [128, 8], F32)
        tin1 = sb("tin1", [1, 8], F32)
        pbuf = sb("pbuf", [128, INNER], F32)
        sg_slab = sb("sg_slab", [128, 512], F32)
        sg2 = [sg_slab[:, 0:512], epi[:, 0:512]]
        pqroll = sb("pqroll", [128, 1024], BF16)
        pqstage = sb("pqstage", [128, INNER], BF16)
        xqF_t = [sb("xqF_t0", [128, 8 * 128], BF16), sb("xqF_t1", [128, 8 * 128], BF16)]
        dvecs = sb("dvecs", [128, 4], F32)
        wchunk = [pbuf[:, j * D:(j + 1) * D] for j in range(4)]

        def accsup(c0, w):
            return pbuf[:, c0:c0 + w]

        ATTN_BASE = {"g": 0, "v": 8192, "d": 16384}

        def attn_lhs(mat, k, m):
            b = ATTN_BASE[mat] + k * D + m * 128
            return wqg[:, b:b + 128]

        def xnF(c0, w):
            return wqu[:, c0:c0 + w]

        def xqF(c0, w):
            return wqu[:, 2048 + c0:2048 + c0 + w]

        def xnbf(c0, w):
            return wqu[:, 4096 + c0:4096 + c0 + w]

        def wdT_build(c0, w):
            return wqu[:, c0:c0 + w]

        def pqF_sup(c0, w):
            return wqg[:, c0:c0 + w]

        def wdT_str(j):
            return wqg[:, 8192 + j * D:8192 + (j + 1) * D]

        pb = [ps(f"pb{i}", [128, 512], F32) for i in range(6)]
        ptb = [ps("ptb0", [128, 1024], BF16), ps("ptb1", [128, 1024], BF16)]

        names = ["G", "T", "V", "S", "LA0", "LX0", "LX1", "LP0", "LP1",
                 "LQ0", "LQ1", "LB0", "LB1", "LW0", "LW1", "LW2", "LW3",
                 "LP2", "LP3", "LW4", "LW5", "LW6", "LW7",
                 "SA", "SX", "SW", "SP", "SO0", "SO1"]
        sems = {n: SemC(n) for n in names}
        for smc in sems.values():
            smc.h = es.enter_context(nc.semaphore(smc.name))
        G, T, V, S = sems["G"], sems["T"], sems["V"], sems["S"]
        LA0 = sems["LA0"]
        LX = [sems["LX0"], sems["LX1"]]
        LP = [sems["LP0"], sems["LP1"], sems["LP2"], sems["LP3"]]
        LQ = [sems["LQ0"], sems["LQ1"]]
        LB = [sems["LB0"], sems["LB1"]]
        LW = [sems[f"LW{j}"] for j in range(8)]
        SA, SX, SW, SP = sems["SA"], sems["SX"], sems["SW"], sems["SP"]
        SO = [sems["SO0"], sems["SO1"]]

        block = es.enter_context(nc.Block())
        plan = {"t": [], "v": [], "s": [], "g": [], "y": []}
        ENG_SEM = {"v": V, "s": S}

        def op(eng, build, sem=None, inc=0, wait=()):
            waits = list(wait)
            if eng in ("v", "s"):
                esem = ENG_SEM[eng]
                if esem.n > 0:
                    waits.append((esem, esem.n))
                sem, inc = esem, 1
            for (smc, val) in waits:
                if val > 0:
                    plan[eng].append(lambda e, _s=smc, _v=val: e.wait_ge(_s.h, _v))
            if sem is not None and inc:
                sem.n += inc

            def f(e, _b=build, _s=sem, _i=inc):
                ins = _b(e)
                if _s is not None and _i:
                    ins.then_inc(_s.h, _i)
            plan[eng].append(f)

        def wv(smc):
            return (smc, smc.n)

        C_NH = consts_sb[:, 0:1]
        C_PH = consts_sb[:, 1:2]
        C_MG = consts_sb[:, 2:3]
        C_NMG = consts_sb[:, 3:4]
        C_EPS = consts_sb[:, 4:5]

        def emit_round_neg(src_ap, s_vec_ap, dst_ap, clip1=False, pre_wait=()):
            fd = src_ap.shape[-1]
            tq, ci, cm, nd, b1 = (qt[:, 0:fd], qci[:, 0:fd], qcm[:, 0:fd],
                                  qnd[:, 0:fd], qg1[:, 0:fd])
            if s_vec_ap is None:
                op("v", lambda e, a=src_ap, o=tq: e.tensor_copy(o, a), wait=pre_wait)
            else:
                op("v", lambda e, a=src_ap, sv=s_vec_ap, o=tq:
                   e.tensor_scalar(o, a, sv, None, OP.mult), wait=pre_wait)
            if clip1:
                op("v", lambda e, a=nhalf[:, 0:fd], b=tq, o=b1:
                   e.tensor_tensor(o, a, b, OP.is_gt))
                op("v", lambda e, a=tq, b=phalf[:, 0:fd], o=cm:
                   e.tensor_tensor(o, a, b, OP.is_gt))
                op("v", lambda e, a=b1, b=cm, o=dst_ap: e.tensor_tensor(o, a, b, OP.subtract))
                return
            op("v", lambda e, a=tq, o=ci: e.tensor_scalar(o, a, MAGIC, None, OP.add))
            op("v", lambda e, a=ci, o=cm: e.tensor_scalar(o, a, -MAGIC, None, OP.add))
            op("v", lambda e, a=cm, b=tq, o=nd: e.tensor_tensor(o, a, b, OP.subtract))
            op("v", lambda e, a=nhalf[:, 0:fd], b=nd, o=b1:
               e.tensor_tensor(o, a, b, OP.is_gt))
            op("v", lambda e, a=nd, b=phalf[:, 0:fd], o=tq:
               e.tensor_tensor(o, a, b, OP.is_gt))
            op("v", lambda e, a=tq, b=b1, o=nd: e.tensor_tensor(o, a, b, OP.subtract))
            op("v", lambda e, a=nd, b=cm, o=dst_ap: e.tensor_tensor(o, a, b, OP.subtract))

        tp_state = {"n": 0, "marks": []}

        def emit_transpose4(srcs, dst_ap, src_wait=(), dst_wait=()):
            n = tp_state["n"]
            tp_state["n"] += 1
            bank = n % 2
            half = (n // 2) % 2
            base = half * 4
            twait = list(src_wait)
            if n - 2 >= 0:
                twait.append(tp_state["marks"][n - 2])
            for i, s4 in enumerate(srcs):
                op("t", lambda e, i_=s4, o=ptb[bank][:, (base + i) * 128:(base + i + 1) * 128]:
                   e.transpose(o, i_, idb[:]),
                   sem=T, inc=(1 if i == 3 else 0),
                   wait=tuple(twait) if i == 0 else ())
            op("s", lambda e, i_=ptb[bank][:, base * 128:(base + 4) * 128], o=dst_ap:
               e.activation(o, i_, AF.Copy, bias=0.0, scale=1.0),
               wait=(wv(T),) + tuple(dst_wait))
            tp_state["marks"].append(wv(S))

        for dst, src in ((consts_sb, d_consts), (idf, d_idf), (idb, d_idb),
                         (ones128, d_ones), (onesT, d_onesT), (bias_sb, d_bias)):
            op("g", lambda e, o=dst, i=src: e.dma_start(out=o[:], in_=i), sem=G, inc=16)
        for nm, src in (("g", d_Wg), ("v", d_Wv), ("d", d_Wd)):
            for k in range(8):
                b = ATTN_BASE[nm] + k * D
                op("g", lambda e, o=wqg[:, b:b + D], i=src[k * 128:(k + 1) * 128, :]:
                   e.dma_start(out=o, in_=i), sem=G, inc=16)
        g_after_consts = G.n

        op("v", lambda e: e.memset(carry[:], 0.0))
        op("v", lambda e: e.memset(nhalf[:], -0.5))
        op("v", lambda e: e.memset(phalf[:], 0.5))

        for i in range(NT):
            buf, sq = wchunk[0], wchunk[1]
            op("y", lambda e, o=buf, i_=d_x[i * 128:(i + 1) * 128, :]:
               e.dma_start(out=o[:], in_=i_), sem=LA0, inc=16, wait=(wv(V), wv(S)))
            ldw = wv(LA0)
            op("v", lambda e, i_=buf, o=qt[:, i:i + 1]:
               e.tensor_reduce(o, i_[:], AX.X, OP.add), wait=(ldw,))
            op("s", lambda e, i_=buf, o=sq: e.activation(o[:], i_[:], AF.Square, bias=0.0, scale=1.0),
               wait=(ldw, wv(V)))
            op("v", lambda e, i_=sq, o=qcm[:, i:i + 1]:
               e.tensor_reduce(o, i_[:], AX.X, OP.add), wait=(wv(S),))
        op("v", lambda e: e.tensor_scalar(meanall[:], qt[:, 0:NT], 1.0 / D, None, OP.mult))
        op("v", lambda e: e.tensor_scalar(qnd[:, 0:NT], qcm[:, 0:NT], 1.0 / D, None, OP.mult))
        op("v", lambda e: e.tensor_tensor(qg1[:, 0:NT], meanall[:], meanall[:], OP.mult))
        op("v", lambda e: e.tensor_tensor(varall[:], qnd[:, 0:NT], qg1[:, 0:NT], OP.subtract))
        op("s", lambda e: e.activation(qt[:, 32:32 + NT], varall[:], AF.Sqrt, bias=C_EPS, scale=1.0),
           wait=(wv(V),))
        op("v", lambda e: e.reciprocal(qcm[:, 32:32 + NT], qt[:, 32:32 + NT]), wait=(wv(S),))
        op("v", lambda e: e.tensor_tensor(qnd[:, 32:32 + NT], qcm[:, 32:32 + NT],
                                          qcm[:, 32:32 + NT], OP.mult))
        op("v", lambda e: e.tensor_scalar(qg1[:, 32:32 + NT], varall[:], 1e-5, None, OP.add))
        op("v", lambda e: e.tensor_tensor(qt[:, 64:64 + NT], qnd[:, 32:32 + NT],
                                          qg1[:, 32:32 + NT], OP.mult))
        op("v", lambda e: e.tensor_scalar(qcm[:, 64:64 + NT], qt[:, 64:64 + NT],
                                          -0.5, 1.5, OP.mult, OP.add))
        op("v", lambda e: e.tensor_tensor(rstdall[:], qcm[:, 32:32 + NT],
                                          qcm[:, 64:64 + NT], OP.mult))

        sa_group = {}
        sx_group = {}
        hbank_mark = [None, None]
        act_marks = [None, None]
        xpose_mark = None
        xg_read_done = {}
        for g in range(NG):
            xg = x_grp[g % 2]
            lx = LX[g % 2]
            xl_wait = (xg_read_done[g - 2],) if g >= 2 else (wv(V),)
            for tt in range(2):
                op("y", lambda e, o=xg[:, tt * D:(tt + 1) * D],
                   i_=d_x[g * GRP + tt * 128:g * GRP + (tt + 1) * 128, :]:
                   e.dma_start(out=o, in_=i_),
                   sem=lx, inc=16, wait=xl_wait if tt == 0 else ())
            ld_x = wv(lx)
            if g > 0:
                op("v", lambda e: e.nop(), wait=((SA, sa_group[g - 1]), (SX, sx_group[g - 1])))
                op("s", lambda e: e.nop(), wait=((SA, sa_group[g - 1]), (SX, sx_group[g - 1])))

            for tt in range(2):
                idx = 2 * g + tt
                xtile = xg[:, tt * D:(tt + 1) * D]
                xnb = xnbf(tt * D, D)
                ln_wait = [ld_x]
                if xpose_mark is not None and tt == 0:
                    ln_wait.append(xpose_mark)
                op("v", lambda e, i_=xtile, o=xnb, m=meanall[:, idx:idx + 1],
                   r=rstdall[:, idx:idx + 1]:
                   e.tensor_scalar(o, i_, m, r, OP.subtract, OP.mult),
                   wait=tuple(ln_wait))
                mx, mc, rec, sv = scal[:, 0:1], scal[:, 1:2], scal[:, 2:3], scal[:, 3:4]
                op("v", lambda e, i_=xnb, o=mx:
                   e.tensor_reduce(o, i_, AX.X, OP.max, apply_absolute_value=True))
                op("v", lambda e, i_=mx, o=mc: e.tensor_scalar(o, i_, 1e-5, None, OP.max))
                op("v", lambda e, i_=mc, o=rec: e.reciprocal(o, i_))
                op("v", lambda e, i_=rec, o=sv: e.tensor_scalar(o, i_, 127.0, None, OP.mult))
                op("v", lambda e, i_=mc, o=dxall[:, idx:idx + 1]:
                   e.tensor_scalar(o, i_, 1.0 / 127.0, None, OP.mult))
                for pc in range(2):
                    qw = (xpose_mark,) if (xpose_mark is not None and tt == 0 and pc == 0) else ()
                    emit_round_neg(xnb[:, pc * 512:(pc + 1) * 512], sv,
                                   pqstage[:, tt * D + pc * 512:tt * D + (pc + 1) * 512],
                                   pre_wait=qw)
            srcw = (wv(V),)
            for kp in range(4):
                k = 2 * kp
                emit_transpose4(
                    [xnbf(0 * D + k * 128, 128), xnbf(1 * D + k * 128, 128),
                     xnbf(0 * D + (k + 1) * 128, 128), xnbf(1 * D + (k + 1) * 128, 128)],
                    xnF(k * GRP, 512), src_wait=srcw)
            for kp in range(4):
                k = 2 * kp
                emit_transpose4(
                    [pqstage[:, 0 * D + k * 128:0 * D + (k + 1) * 128],
                     pqstage[:, 1 * D + k * 128:1 * D + (k + 1) * 128],
                     pqstage[:, 0 * D + (k + 1) * 128:0 * D + (k + 2) * 128],
                     pqstage[:, 1 * D + (k + 1) * 128:1 * D + (k + 2) * 128]],
                    xqF(k * GRP, 512), src_wait=srcw)
            xpose_mark = wv(T)

            xn_evac_marks = (wv(V), wv(S))
            sd_mark = None
            eps_mark = None
            scan_marks = {}

            def emit_htrans_stage(mh):
                for tt in range(2):
                    pdst2 = pb[4 + tt][:, 0:128]
                    tw = [scan_marks[mh]]
                    if hbank_mark[tt] is not None:
                        tw.append(hbank_mark[tt])
                    op("t", lambda e, i_=epi[:, 5 * GRP + tt * 128:5 * GRP + (tt + 1) * 128], o=pdst2:
                       e.transpose(o, i_, idf[:]), sem=T, inc=1, wait=tuple(tw))
                    hmark = wv(T)
                    stage = (qt, qcm, qnd, qg1)[tt * 2 + (mh // 4)]
                    scol = (mh % 4) * 128
                    op("v", lambda e, i_=pdst2, o=stage[:, scol:scol + 128],
                       x_=xg[:, tt * D + mh * 128: tt * D + mh * 128 + 128]:
                       e.tensor_tensor(o, i_, x_, OP.add), wait=(hmark,))
                    hbank_mark[tt] = wv(V)

            def emit_acc_dma(h2):
                vd = wv(V)
                for tt in range(2):
                    stage = (qt, qcm, qnd, qg1)[tt * 2 + h2]
                    rows = g * GRP + tt * 128
                    op("y", lambda e, i_=stage,
                       o=scr_acc[rows:rows + 128, h2 * 512:(h2 + 1) * 512]:
                       e.dma_start(out=o, in_=i_[:]), sem=SA, inc=16, wait=(vd,))

            for m in range(8):
                par = m % 2
                pg = pb[2 * par][:, 0:GRP]
                pv = pb[2 * par][:, GRP:2 * GRP]
                pd = pb[2 * par + 1][:, 0:GRP]
                mm_wait = list(xn_evac_marks) if m < 2 else []
                if act_marks[par] is not None:
                    mm_wait.append(act_marks[par])
                mm_wait = tuple(mm_wait)
                first = True
                for mat, pdst in (("g", pg), ("v", pv), ("d", pd)):
                    for k in range(8):
                        op("t", lambda e, l=attn_lhs(mat, k, m),
                           r=xnF(k * GRP, GRP), o=pdst, st=(k == 0), sp=(k == 7):
                           e.matmul(o, l, r, start=st, stop=sp),
                           sem=T, inc=(1 if k == 7 else 0),
                           wait=mm_wait if first else ())
                        first = False
                t_gvd = wv(T)
                if m >= 1:
                    emit_htrans_stage(m - 1)
                    if m == 4:
                        emit_acc_dma(0)
                op("s", lambda e, i_=pg, o=sgb, b=bias_sb[:, m:m + 1]:
                   e.activation(o, i_, AF.Sigmoid, bias=b, scale=1.0),
                   wait=(t_gvd,) if eps_mark is None else (t_gvd, eps_mark))
                op("s", lambda e, i_=pv, o=thb, b=bias_sb[:, 8 + m:9 + m]:
                   e.activation(o, i_, AF.Tanh, bias=b, scale=1.0))
                op("s", lambda e, i_=pd, o=sdb, b=bias_sb[:, 16 + m:17 + m]:
                   e.activation(o, i_, AF.Sigmoid, bias=b, scale=1.0))
                sd_mark = wv(S)
                act_marks[par] = sd_mark
                op("v", lambda e: e.tensor_tensor(xsb, sgb,
                                                  thb, OP.mult), wait=(sd_mark,))
                op("v", lambda e: e.tensor_scalar(ab, sdb,
                                                  0.998, 0.001, OP.mult, OP.add))
                eps_mark = wv(V)
                op("v", lambda e, c=carry[:, m:m + 1]: e.tensor_tensor_scan(
                    hb, ab, xsb, c, OP.mult, OP.add))
                op("v", lambda e, c=carry[:, m:m + 1]: e.tensor_copy(c, epi[:, 6 * GRP - 1:6 * GRP]))
                scan_marks[m] = wv(V)
            emit_htrans_stage(7)
            xg_read_done[g] = wv(V)
            emit_acc_dma(1)

            v_done, s_done = wv(V), wv(S)
            for k in range(8):
                op("y", lambda e, i_=xqF(k * GRP, GRP), o=scr_xqF[k, :, g * GRP:(g + 1) * GRP]:
                   e.dma_start(out=o, in_=i_), sem=SX, inc=16,
                   wait=(s_done, v_done) if k == 0 else ())
            sa_group[g] = SA.n
            sx_group[g] = SX.n

        op("v", lambda e: e.nop(), wait=((SA, sa_group[NG - 1]), (SX, sx_group[NG - 1])))
        op("s", lambda e: e.nop(), wait=((SA, sa_group[NG - 1]), (SX, sx_group[NG - 1])))

        def prep_scalars(sw_col, dw_col, inv_n):
            op("t", lambda e: e.matmul(pb[5][0:1, 0:1], ones128[:], scal[:, 4:5],
                                       start=True, stop=True), sem=T, inc=1, wait=(wv(V),))
            op("v", lambda e: e.tensor_copy(tin1[0:1, 0:1], pb[5][0:1, 0:1]), wait=(wv(T),))
            op("v", lambda e: e.tensor_scalar(tin1[0:1, 1:2], tin1[0:1, 0:1], inv_n, None, OP.mult))
            op("v", lambda e: e.tensor_scalar(tin1[0:1, 2:3], tin1[0:1, 1:2], 1e-5, None, OP.max))
            op("v", lambda e: e.reciprocal(tin1[0:1, 3:4], tin1[0:1, 2:3]))
            op("v", lambda e: e.reciprocal(tin1[0:1, 4:5], tin1[0:1, 3:4]))
            for srcc, col in ((3, sw_col), (4, dw_col)):
                op("t", lambda e, i_=tin1[0:1, srcc:srcc + 1]:
                   e.matmul(pb[5][:, 1:2], onesT[:], i_, start=True, stop=True),
                   sem=T, inc=1, wait=(wv(V),))
                op("v", lambda e, o=wsc[:, col:col + 1]: e.tensor_copy(o, pb[5][:, 1:2]),
                   wait=(wv(T),))

        def prep_mean(load_list):
            npart = len(load_list) * 4
            assert npart <= 512
            vmarks = {}
            for li, src_slice in enumerate(load_list):
                buf = wchunk[li % 4]
                lp = LP[li % 4]
                lw_ = (vmarks[li - 4],) if li >= 4 else (wv(V),)
                op("y", lambda e, o=buf, i_=src_slice: e.dma_start(out=o, in_=i_),
                   sem=lp, inc=16, wait=lw_)
                ldw = wv(lp)
                for pc in range(4):
                    col = li * 4 + pc
                    op("v", lambda e, i_=buf[:, pc * 256:(pc + 1) * 256], o=qg1[:, col:col + 1]:
                       e.tensor_reduce(o, i_, AX.X, OP.add, apply_absolute_value=True),
                       wait=(ldw,) if pc == 0 else ())
                vmarks[li] = wv(V)
            op("v", lambda e, w=npart: e.tensor_reduce(scal[:, 4:5], qg1[:, 0:w], AX.X, OP.add))

        def qtmp4(cc, c0, w):
            return pqstage[:, cc * D + c0:cc * D + c0 + w]

        def prep_quant(load_list, sw_col, dest4_fn, pre_first=()):
            vmarks = {}
            tmark_prev_batch = None
            for bi in range(len(load_list) // 4):
                batch = load_list[bi * 4:(bi + 1) * 4]
                for cc, (src_slice, _) in enumerate(batch):
                    li = bi * 4 + cc
                    buf = wchunk[li % 4]
                    lp = LP[li % 4]
                    lw_ = (vmarks[li - 4],) if li >= 4 else (wv(V),)
                    op("y", lambda e, o=buf, i_=src_slice: e.dma_start(out=o, in_=i_),
                       sem=lp, inc=16, wait=lw_)
                    ldw = wv(lp)
                    for pc in range(2):
                        pw = [ldw] if pc == 0 else []
                        if pc == 0 and tmark_prev_batch is not None:
                            pw.append(tmark_prev_batch)
                        if li == 0:
                            pw.extend(pre_first)
                        emit_round_neg(buf[:, pc * 512:(pc + 1) * 512],
                                       wsc[:, sw_col:sw_col + 1],
                                       qtmp4(cc, pc * 512, 512), clip1=True,
                                       pre_wait=tuple(pw))
                    vmarks[li] = wv(V)
                srcw = (wv(V),)
                metas = [m for (_, m) in batch]
                for kb in range(8):
                    dest4_fn(metas, kb, srcw)
                tmark_prev_batch = wv(T)

        down_l1 = [d_wdown[c * 128:(c + 1) * 128, q * D:(q + 1) * D]
                   for c in range(8) for q in range(4)]
        prep_mean(down_l1)
        prep_scalars(4, 5, 1.0 / (D * INNER))
        down_l2 = [(d_wdown[(cb * 4 + cc) * 128:(cb * 4 + cc + 1) * 128, q * D:(q + 1) * D],
                    (cb * 4 + cc, q))
                   for q in range(4) for cb in range(2) for cc in range(4)]

        def down_dest4(metas, kb, srcw):
            c0, q = metas[0]
            kglob = q * 8 + kb
            emit_transpose4([qtmp4(cc, kb * 128, 128) for cc in range(4)],
                            wdT_build(kglob * D + c0 * 128, 512), src_wait=srcw)
        prep_quant(down_l2, 4, down_dest4)
        v_done, s_done = wv(V), wv(S)
        for k in range(32):
            op("y", lambda e, i_=wdT_build(k * D, D), o=scr_wdT[k]:
               e.dma_start(out=o, in_=i_), sem=SW, inc=16,
               wait=(v_done, s_done) if k == 0 else ())
        st_wdT = SW.n

        gate_l1 = [d_wgate[c * 128:(c + 1) * 128, :] for c in range(32)]
        prep_mean(gate_l1)
        prep_scalars(0, 1, 1.0 / (INNER * D))
        gate_l2 = [(d_wgate[c * 128:(c + 1) * 128, :], c) for c in range(32)]

        def gate_dest4(metas, kb, srcw):
            c0 = metas[0]
            emit_transpose4([qtmp4(cc, kb * 128, 128) for cc in range(4)],
                            wqg[:, kb * INNER + c0 * 128: kb * INNER + c0 * 128 + 512],
                            src_wait=srcw)
        prep_quant(gate_l2, 0, gate_dest4)

        up_l1 = [d_wup[c * 128:(c + 1) * 128, :] for c in range(32)]
        prep_mean(up_l1)
        prep_scalars(2, 3, 1.0 / (INNER * D))
        up_l2 = [(d_wup[c * 128:(c + 1) * 128, :], c) for c in range(32)]

        def up_dest4(metas, kb, srcw):
            c0 = metas[0]
            emit_transpose4([qtmp4(cc, kb * 128, 128) for cc in range(4)],
                            wqu[:, kb * INNER + c0 * 128: kb * INNER + c0 * 128 + 512],
                            src_wait=srcw)
        prep_quant(up_l2, 2, up_dest4, pre_first=((SW, st_wdT),))

        slab_ctr = {"n": 0}
        st_pq = {}
        t_after_tile = {}
        bank_marks = [None] * 6
        roll_marks = []
        sg2_marks = [None, None]
        for i in range(NT):
            xqt = xqF_t[i % 2]
            lq = LQ[i % 2]
            xq_wait = [(SX, sx_group[i // 2])]
            if i >= 2:
                xq_wait.append((T, t_after_tile[i - 2]))
            else:
                xq_wait.append(wv(T))
            for k in range(8):
                op("y", lambda e, o=xqt[:, k * 128:(k + 1) * 128],
                   i_=scr_xqF[k, :, i * 128:(i + 1) * 128]:
                   e.dma_start(out=o, in_=i_), sem=lq, inc=16,
                   wait=tuple(xq_wait) if k == 0 else ())
            ld_xq = wv(lq)
            op("v", lambda e, o=dvecs[:, 0:1], a=dxall[:, i:i + 1], b=wsc[:, 1:2]:
               e.tensor_tensor(o, a, b, OP.mult), wait=(wv(S),))
            op("v", lambda e, o=dvecs[:, 1:2], a=dxall[:, i:i + 1], b=wsc[:, 3:4]:
               e.tensor_tensor(o, a, b, OP.mult))
            dv_mark = wv(V)
            if i > 0:
                op("v", lambda e: e.nop(), wait=((SP, st_pq[i - 1]),))
                op("s", lambda e: e.nop(), wait=((SP, st_pq[i - 1]),))

            for s_ in range(8):
                bkg = (2 * slab_ctr["n"]) % 6
                bku = (2 * slab_ctr["n"] + 1) % 6
                slab_ctr["n"] += 1
                pg_, pu_ = pb[bkg][:], pb[bku][:]
                mmw = [ld_xq] if s_ == 0 else []
                for bk in (bkg, bku):
                    if bank_marks[bk] is not None:
                        mmw.append(bank_marks[bk])
                for k in range(8):
                    bg0 = k * INNER + s_ * 512
                    op("t", lambda e, l=xqt[:, k * 128:(k + 1) * 128],
                       r=wqg[:, bg0:bg0 + 512], o=pg_, st=(k == 0), sp=(k == 7):
                       e.matmul(o, l, r, start=st, stop=sp),
                       sem=T, inc=(1 if k == 7 else 0), wait=tuple(mmw) if k == 0 else ())
                    op("t", lambda e, l=xqt[:, k * 128:(k + 1) * 128],
                       r=wqu[:, bg0:bg0 + 512], o=pu_, st=(k == 0), sp=(k == 7):
                       e.matmul(o, l, r, start=st, stop=sp),
                       sem=T, inc=(1 if k == 7 else 0))
                tmm = wv(T)
                spar = slab_ctr["n"] % 2
                sw = [tmm]
                if sg2_marks[spar] is not None:
                    sw.append(sg2_marks[spar])
                if s_ == 0:
                    sw.append(dv_mark)
                op("s", lambda e, i_=pg_, o=sg2[spar], sc=dvecs[:, 0:1]:
                   e.activation(o, i_, AF.Silu, bias=0.0, scale=sc), wait=tuple(sw))
                silu_mark = wv(S)
                bank_marks[bkg] = silu_mark
                op("v", lambda e, i_=pu_, o=pbuf[:, s_ * 512:(s_ + 1) * 512], sc=dvecs[:, 1:2],
                   sg=sg2[spar]:
                   e.scalar_tensor_tensor(o, i_, sc, sg, OP.mult, OP.mult),
                   wait=(silu_mark,))
                sg2_marks[spar] = wv(V)
                bank_marks[bku] = wv(V)
            mxp, mcp, recp, svp = scal[:, 6:7], scal[:, 7:8], scal[:, 0:1], scal[:, 1:2]
            op("v", lambda e, o=mxp: e.tensor_reduce(o, pbuf[:], AX.X, OP.max,
                                                     apply_absolute_value=True))
            op("v", lambda e, i_=mxp, o=mcp: e.tensor_scalar(o, i_, 1e-5, None, OP.max))
            op("v", lambda e, i_=mcp, o=recp: e.reciprocal(o, i_))
            op("v", lambda e, i_=recp, o=svp: e.tensor_scalar(o, i_, 127.0, None, OP.mult))
            op("v", lambda e, i_=mcp, o=scal[:, 2:3]:
               e.tensor_scalar(o, i_, 1.0 / 127.0, None, OP.mult))
            op("v", lambda e, o=dpwall[:, i:i + 1], a=scal[:, 2:3], b=wsc[:, 5:6]:
               e.tensor_tensor(o, a, b, OP.mult))
            for pc in range(8):
                dst_roll = pqroll[:, (pc % 2) * 512:(pc % 2) * 512 + 512]
                pw = (roll_marks[-2],) if len(roll_marks) >= 2 else ()
                emit_round_neg(pbuf[:, pc * 512:(pc + 1) * 512], svp, dst_roll,
                               pre_wait=pw)
                emit_transpose4([dst_roll[:, b4 * 128:(b4 + 1) * 128] for b4 in range(4)],
                                pqstage[:, pc * 512:(pc + 1) * 512], src_wait=(wv(V),))
                roll_marks.append(wv(T))
            t_after_tile[i] = T.n
            v_done, s_done = wv(V), wv(S)
            op("y", lambda e, i_=pqstage, o=scr_pqF[i]:
               e.dma_start(out=o, in_=i_[:]), sem=SP, inc=16, wait=(v_done, s_done))
            st_pq[i] = SP.n

        b1_done_T = wv(T)
        b1_done_V = wv(V)
        t_tile_mark = {}
        t_k_mark = {}
        pair_marks = [None, None, None]
        so_marks = {0: [], 1: []}
        v_evac_marks = {}

        def wdT_str4(j):
            return wqg[:, 8192 + j * D:8192 + (j + 1) * D]

        for u in range(NSUP):
            tiles = (2 * u, 2 * u + 1)
            lb = LB[u % 2]
            pr0 = (2 * u) % 3
            pr1 = (2 * u + 1) % 3
            wl = [b1_done_T, b1_done_V, (SP, st_pq[tiles[1]])]
            if u > 0:
                wl.append((V, v_evac_marks[u - 1]))
                wl.append((T, t_tile_mark[(u - 1, 1)]))
            for j, ti in enumerate(tiles):
                op("y", lambda e, o=pqF_sup(j * INNER, INNER), i_=scr_pqF[ti]:
                   e.dma_start(out=o, in_=i_), sem=lb, inc=16,
                   wait=tuple(wl) if j == 0 else ())
            for tt2 in range(2):
                op("y", lambda e, o=accsup(tt2 * D, D),
                   i_=scr_acc[u * SUP + tt2 * 128:u * SUP + (tt2 + 1) * 128, :]:
                   e.dma_start(out=o, in_=i_),
                   sem=lb, inc=16, wait=((SA, sa_group[u]),) if tt2 == 0 else ())
            ld_sup = wv(lb)
            for k in range(32):
                kg = u * 32 + k
                wbuf = wdT_str4(kg % 8)
                lw = LW[kg % 8]
                w_wait = [(SW, st_wdT)]
                if kg - 8 >= 0:
                    uu, kk = divmod(kg - 8, 32)
                    w_wait.append((T, t_k_mark[(uu, kk)]))
                else:
                    w_wait.append(b1_done_T)
                op("y", lambda e, o=wbuf, i_=scr_wdT[k]: e.dma_start(out=o, in_=i_),
                   sem=lw, inc=16, wait=tuple(w_wait))
                ld_w = wv(lw)
                for j in range(2):
                    pr = pr0 if j == 0 else pr1
                    for h in range(2):
                        pdst = pb[2 * pr + h][:]
                        mmw = []
                        if k == 0 and h == 0:
                            mmw = [ld_sup, ld_w]
                            if pair_marks[pr] is not None:
                                mmw.append(pair_marks[pr])
                        elif j == 0 and h == 0:
                            mmw = [ld_w]
                        op("t", lambda e, l=pqF_sup(j * INNER + k * 128, 128),
                           r=wbuf[:, h * 512:(h + 1) * 512], o=pdst,
                           st=(k == 0), sp=(k == 31):
                           e.matmul(o, l, r, start=st, stop=sp),
                           sem=T, inc=(1 if h == 1 else 0),
                           wait=tuple(mmw))
                    if k == 31:
                        t_tile_mark[(u, j)] = T.n
                t_k_mark[(u, k)] = T.n
            for j, ti in enumerate(tiles):
                pr = pr0 if j == 0 else pr1
                tmark = (T, t_tile_mark[(u, j)])
                for h in range(2):
                    pidx = 2 * j + h
                    psrc = pb[2 * pr + h][:]
                    spar = pidx % 2
                    stg = sg_slab if spar == 0 else qg1
                    stw = [tmark]
                    if so_marks[spar]:
                        stw.append((SO[spar], so_marks[spar][-1]))
                    op("v", lambda e, i_=psrc, o=qt[:], sc=dpwall[:, ti:ti + 1]:
                       e.tensor_scalar(o, i_, sc, None, OP.mult), wait=tuple(stw))
                    op("v", lambda e, a=qt[:], b=accsup(j * D + h * 512, 512), o=stg[:]:
                       e.tensor_tensor(o, a, b, OP.add))
                    vmark = wv(V)
                    rows = u * SUP + j * 128
                    op("y", lambda e, i_=stg,
                       o=d_out[rows:rows + 128, h * 512:(h + 1) * 512]:
                       e.dma_start(out=o, in_=i_[:]), sem=SO[spar], inc=16, wait=(vmark,))
                    so_marks[spar].append(SO[spar].n)
                pair_marks[pr] = wv(V)
            v_evac_marks[u] = V.n

        fin0, fin1 = SO[0].n, SO[1].n
        op("y", lambda e, a=fin0, b=fin1: e.wait_ge(SO[0].h, a))
        op("y", lambda e, a=fin0, b=fin1: e.wait_ge(SO[1].h, b))

        @block.sync
        def _(e):
            for f in plan["y"]:
                f(e)

        @block.gpsimd
        def _(e):
            for f in plan["g"]:
                f(e)

        @block.vector
        def _(e):
            e.wait_ge(G.h, g_after_consts)
            for f in plan["v"]:
                f(e)

        @block.scalar
        def _(e):
            e.wait_ge(G.h, g_after_consts)
            for f in plan["s"]:
                f(e)

        @block.tensor
        def _(e):
            e.wait_ge(G.h, g_after_consts)
            for f in plan["t"]:
                f(e)

    return nc


_NC_CACHE = {}


def kernel(L0_attn_x, L0_attn_Wg, L0_attn_bg, L0_attn_Wv, L0_attn_bv,
           L0_attn_Wd, L0_attn_bd, L0_ffn_w_gate, L0_ffn_w_up, L0_ffn_w_down):
    x = np.ascontiguousarray(np.asarray(L0_attn_x, dtype=np.float32))
    B = x.shape[0]
    Wg = np.ascontiguousarray(np.asarray(L0_attn_Wg, np.float32))
    Wv = np.ascontiguousarray(np.asarray(L0_attn_Wv, np.float32))
    Wd = np.ascontiguousarray(np.asarray(L0_attn_Wd, np.float32))
    bg = np.asarray(L0_attn_bg, np.float32).reshape(8, 128).T
    bv = np.asarray(L0_attn_bv, np.float32).reshape(8, 128).T
    bd = np.asarray(L0_attn_bd, np.float32).reshape(8, 128).T
    biasT = np.ascontiguousarray(np.concatenate([bg, bv, bd], axis=1))
    wgate = np.ascontiguousarray(np.asarray(L0_ffn_w_gate, np.float32))
    wup = np.ascontiguousarray(np.asarray(L0_ffn_w_up, np.float32))
    wdown = np.ascontiguousarray(np.asarray(L0_ffn_w_down, np.float32))

    consts = np.zeros((128, 16), np.float32)
    consts[:, 0] = -0.5
    consts[:, 1] = 0.5
    consts[:, 2] = MAGIC
    consts[:, 3] = -MAGIC
    consts[:, 4] = 1e-5
    identf = np.eye(128, dtype=np.float32)
    identb = np.asarray(np.eye(128, dtype=np.float32).astype(ml_dtypes.bfloat16))
    ones128 = np.ones((128, 1), np.float32)
    onesT = np.ones((1, 128), np.float32)

    if "nc" not in _NC_CACHE:
        _NC_CACHE["nc"] = build_program()
    nc = _NC_CACHE["nc"]

    in_maps = []
    for b in range(B):
        in_maps.append(dict(
            x=np.ascontiguousarray(x[b]), Wg=Wg, Wv=Wv, Wd=Wd, biasT=biasT,
            wgate=wgate, wup=wup, wdown=wdown, consts=consts,
            identf=identf, identb=identb, ones128=ones128, onesT=onesT,
        ))
    res = run_bass_kernel_spmd(nc, in_maps, core_ids=list(range(B)))
    out = np.stack([np.asarray(r["out"], np.float32) for r in res.results], axis=0)
    return out
